# version 02 (final)
# baseline (speedup 1.0000x reference)
import numpy as np
import concourse.bass as bass
import concourse.mybir as mybir
from concourse.bass_utils import run_bass_kernel_spmd
from concourse.bass_core_rust import IndirectOffsetOnAxis
from contextlib import ExitStack

F32 = mybir.dt.float32
BF16 = mybir.dt.bfloat16
U32 = mybir.dt.uint32
AF = mybir.ActivationFunctionType
ALU = mybir.AluOpType
AX = mybir.AxisListType

NCORES = 8
B, S, D = 16, 2048, 1024
BPC = B // NCORES
TPC = BPC * S


class P:

    ENG = ("tensor", "vector", "scalar", "gpsimd", "sync")

    def __init__(self, nc):
        self.nc = nc
        self.st = ExitStack()
        self.queues = {e: [] for e in self.ENG}
        self.issued = 0
        self.fence_val = 0
        self.waited = {e: 0 for e in self.ENG}
        self.sem = self.st.enter_context(nc.semaphore("fence_sem"))
        self._uid = 0

    def sbuf(self, shape, dtype=F32, name=None):
        self._uid += 1
        name = name or f"sb{self._uid}"
        return self.st.enter_context(self.nc.sbuf_tensor(name, shape, dtype)).ap()

    def sbuf_scoped(self, shape, dtype=F32):
        self._uid += 1
        g = self.nc.sbuf_tensor(f"sc{self._uid}", shape, dtype)
        ap = g.__enter__().ap()
        if not hasattr(self, "_scoped"):
            self._scoped = {}
        self._scoped[id(ap)] = (self._uid, g)
        return ap

    def release(self, *aps):
        items = sorted((self._scoped.pop(id(ap)) for ap in aps),
                       key=lambda x: -x[0])
        for _, g in items:
            g.__exit__(None, None, None)

    def psum(self, shape, dtype=F32, name=None):
        self._uid += 1
        name = name or f"ps{self._uid}"
        return self.st.enter_context(self.nc.psum_tensor(name, shape, dtype)).ap()

    def _emit(self, eng, fn, inc):
        fence = self.fence_val
        need_wait = self.waited[eng] < fence
        if need_wait:
            self.waited[eng] = fence
        self.issued += inc
        sem = self.sem

        def thunk(e, fn=fn, fence=fence, inc=inc, need_wait=need_wait):
            if need_wait:
                e.wait_ge(sem, fence)
            inst = fn(e)
            if inc:
                inst.then_inc(sem, inc)

        self.queues[eng].append(thunk)

    def te(self, fn):
        self._emit("tensor", fn, 1)

    def ve(self, fn):
        self._emit("vector", fn, 1)

    def se(self, fn):
        self._emit("scalar", fn, 1)

    def po(self, fn):
        self._emit("gpsimd", fn, 1)

    def dma(self, out, in_, eng="sync"):
        self._emit(eng, lambda e, o=out, i=in_: e.dma_start(out=o, in_=i), 16)

    def idma(self, out, out_off, in_, in_off, bounds=None, op=ALU.bypass):
        def fn(e):
            return e.indirect_dma_start(
                out, out_off, in_, in_off, bounds_check=bounds, oob_is_err=False,
                compute_op=op)
        self._emit("gpsimd", fn, 16)

    def fence(self):
        self.fence_val = self.issued

    def const_tile(self, val, parts=1):
        key = (float(val), parts)
        if not hasattr(self, "_consts"):
            self._consts = {}
        if key not in self._consts:
            t = self.sbuf([parts, 1])
            self.po(lambda e, t=t, v=float(val): e.memset(t, v))
            self._consts[key] = t
        return self._consts[key]

    def mm(self, out, lhsT, rhs, start=True, stop=True):
        self.te(lambda e: e.matmul(out, lhsT, rhs, start=start, stop=stop))

    def transpose(self, out, in_, ident):
        self.te(lambda e: e.transpose(out, in_, ident))

    def act(self, out, in_, func, bias=0.0, scale=1.0):
        self.se(lambda e: e.activation(out, in_, func, bias=bias, scale=scale))

    def finish(self):
        total = self.issued
        for e in self.ENG:
            if self.queues[e]:
                sem = self.sem
                self.queues[e].append(lambda eng, t=total, s=sem: eng.wait_ge(s, t))
        with self.nc.Block() as blk:
            for e in self.ENG:
                if self.queues[e]:
                    getattr(blk, e)(lambda eng, q=tuple(self.queues[e]): [t(eng) for t in q])
        if hasattr(self, "_scoped"):
            for _, g in sorted(self._scoped.values(), key=lambda x: -x[0]):
                g.__exit__(None, None, None)
            self._scoped = {}
        self.st.close()


def _bass():
    return bass.Bass("TRN2", target_bir_lowering=False)


def _inp(nc, name, shape, dtype=F32):
    return nc.dram_tensor(name, list(shape), dtype, kind="ExternalInput").ap()


def _out(nc, name, shape, dtype=F32):
    return nc.dram_tensor(name, list(shape), dtype, kind="ExternalOutput").ap()


def _scr(nc, name, shape, dtype=F32):
    return nc.dram_tensor(name, list(shape), dtype, kind="Internal").ap()


def build_ln(p, xt, out_tiles, consts, tmp, pb0, pb1, pr0, pr1, nd, tw,
             gamma=None, beta=None, eps=1e-5):
    D_ = nd * 128
    for k in range(nd):
        p.mm(pr0, consts["ones_col"], xt[k], start=(k == 0), stop=(k == nd - 1))
        p.act(tmp[k], xt[k], AF.Square)
    p.fence()
    for k in range(nd):
        p.mm(pr1, consts["ones_col"], tmp[k], start=(k == 0), stop=(k == nd - 1))
    p.fence()
    r0, r1, r2 = consts["r0"], consts["r1"], consts["r2"]
    p.act(r0, pr0, AF.Copy, scale=1.0 / D_)
    p.fence()
    p.act(r1, r0, AF.Square)
    p.ve(lambda e: e.tensor_scalar(r2, pr1, 1.0 / D_, None, ALU.mult))
    p.fence()
    p.ve(lambda e: e.tensor_sub(r2, r2, r1))
    p.fence()
    p.act(r1, r2, AF.Ln, bias=p.const_tile(eps))
    p.fence()
    p.act(r2, r1, AF.Exp, scale=-0.5)
    p.fence()
    p.mm(pb0, consts["ones_row"], r0)
    p.mm(pb1, consts["ones_row"], r2)
    p.fence()
    for k in range(nd):
        p.ve(lambda e, k=k: e.tensor_sub(tmp[k], xt[k], pb0))
    p.fence()
    if gamma is None:
        for k in range(nd):
            p.ve(lambda e, k=k: e.tensor_mul(out_tiles[k], tmp[k], pb1))
    else:
        for k in range(nd):
            p.ve(lambda e, k=k: e.tensor_mul(tmp[k], tmp[k], pb1))
        p.fence()
        for k in range(nd):
            p.ve(lambda e, k=k: e.scalar_tensor_tensor(
                out_tiles[k], tmp[k], gamma[k], beta[k].to_broadcast([128, tw]),
                ALU.mult, ALU.add))
    p.fence()


def build_l1(nc):
    xT = _inp(nc, "xT", (D, TPC))
    keyT = _inp(nc, "keyT", (D, 64))
    valT = _inp(nc, "valT", (D, 64))
    outT = _inp(nc, "outT", (64, D))
    td = _inp(nc, "td", (64, 1))
    tf = _inp(nc, "tf", (64, 1))
    tsg = _inp(nc, "tsg", (D, 1))
    lnw = _inp(nc, "lnw", (D, 1))
    lnb = _inp(nc, "lnb", (D, 1))
    xn_o = _out(nc, "xn", (D, TPC))
    s_o = _out(nc, "s", (D, TPC))

    p = P(nc)
    TW = 512
    ND = 8
    xt = [p.sbuf([128, TW]) for _ in range(ND)]
    xr = [p.sbuf([128, TW]) for _ in range(ND)]
    tmp = [p.sbuf([128, TW]) for _ in range(ND)]
    xnt = [p.sbuf([128, TW]) for _ in range(ND)]
    ones_col = p.sbuf([128, 1])
    ones_row = p.sbuf([1, 128])
    r0 = p.sbuf([1, TW]); r1 = p.sbuf([1, TW]); r2 = p.sbuf([1, TW])
    consts = dict(ones_col=ones_col, ones_row=ones_row, r0=r0, r1=r1, r2=r2)
    pr0 = p.psum([1, TW]); pr1 = p.psum([1, TW])
    pb0 = p.psum([128, TW]); pb1 = p.psum([128, TW])
    pk = p.psum([64, TW]); pv = p.psum([64, TW])
    po = [p.psum([128, TW]) for _ in range(2)]

    kw = [p.sbuf([128, 64]) for _ in range(8)]
    vw = [p.sbuf([128, 64]) for _ in range(8)]
    ow = [p.sbuf([64, 128]) for _ in range(8)]
    gsig = [p.sbuf([128, 1]) for _ in range(8)]
    gm1 = [p.sbuf([128, 1]) for _ in range(8)]
    lnw_c = [p.sbuf([128, 1]) for _ in range(8)]
    lnb_c = [p.sbuf([128, 1]) for _ in range(8)]
    wcol = p.sbuf([64, 1])
    ntf = p.sbuf([64, 1])
    Wb = p.sbuf([64, S])
    scanbuf = [p.sbuf([64, S]) for _ in range(2)]
    Sseq = [p.sbuf([64, S]) for _ in range(2)]
    zb = p.sbuf([64, 1])
    wa = p.sbuf([64, TW])
    we = p.sbuf([64, TW])
    st_t = [p.sbuf([128, TW]) for _ in range(2)]

    p.po(lambda e: e.memset(ones_col, 1.0))
    p.po(lambda e: e.memset(ones_row, 1.0))
    p.po(lambda e: e.memset(zb, 0.0))
    p.const_tile(1e-5)
    for k in range(8):
        p.dma(kw[k], keyT[k * 128:(k + 1) * 128, :])
        p.dma(vw[k], valT[k * 128:(k + 1) * 128, :])
        p.dma(ow[k], outT[:, k * 128:(k + 1) * 128])
        p.dma(lnw_c[k], lnw[k * 128:(k + 1) * 128, :])
        p.dma(lnb_c[k], lnb[k * 128:(k + 1) * 128, :])
    tsg_c = lnw_c
    tsgt = [p.sbuf([128, 1]) for _ in range(8)]
    for k in range(8):
        p.dma(tsgt[k], tsg[k * 128:(k + 1) * 128, :])
    p.dma(wcol, td[:, :])
    p.dma(ntf[:, :], tf[:, :])
    p.fence()
    for k in range(8):
        p.act(gsig[k], tsgt[k], AF.Sigmoid)
    p.act(wcol, wcol, AF.Exp)
    p.act(ntf, ntf, AF.Exp)
    p.fence()
    for k in range(8):
        p.ve(lambda e, k=k: e.tensor_scalar(gm1[k], gsig[k], -1.0, 1.0, ALU.mult, ALU.add))
    p.ve(lambda e: e.tensor_scalar(ntf, ntf, -1.0, None, ALU.mult))
    p.ve(lambda e: e.tensor_scalar(Wb, zb.to_broadcast([64, S]), wcol, None, ALU.add))
    p.fence()

    for tc in range(8):
        cols = slice(tc * TW, (tc + 1) * TW)
        for k in range(ND):
            p.dma(xt[k], xT[k * 128:(k + 1) * 128, cols])
        p.fence()
        build_ln(p, xt, xnt, consts, tmp, pb0, pb1, pr0, pr1, ND, TW)
        for k in range(ND):
            p.dma(xn_o[k * 128:(k + 1) * 128, cols], xnt[k])
        p.fence()

    for tc in range(8):
        b = tc // 4
        t0 = (tc % 4) * TW
        cols = slice(b * S + t0, b * S + t0 + TW)
        colr = slice(b * S + (t0 + 1024) % S, b * S + (t0 + 1024) % S + TW)
        for k in range(ND):
            p.dma(xt[k], xn_o[k * 128:(k + 1) * 128, cols])
            p.dma(xr[k], xn_o[k * 128:(k + 1) * 128, colr])
        p.fence()
        for k in range(ND):
            p.ve(lambda e, k=k: e.tensor_scalar(tmp[k], xt[k], gm1[k], None, ALU.mult))
        p.fence()
        for k in range(ND):
            p.ve(lambda e, k=k: e.scalar_tensor_tensor(
                xt[k], xr[k], gsig[k], tmp[k], ALU.mult, ALU.add))
        p.fence()
        build_ln(p, xt, xnt, consts, tmp, pb0, pb1, pr0, pr1, ND, TW,
                 gamma=lnw_c, beta=lnb_c)
        for k in range(ND):
            p.mm(pk, kw[k], xnt[k], start=(k == 0), stop=(k == ND - 1))
        for k in range(ND):
            p.mm(pv, vw[k], xnt[k], start=(k == 0), stop=(k == ND - 1))
        p.fence()
        p.ve(lambda e: e.tensor_scalar(wa, pk, ntf, None, ALU.mult))
        p.fence()
        p.act(we, wa, AF.Exp)
        p.fence()
        p.ve(lambda e, b=b, t0=t0: e.tensor_mul(
            scanbuf[b][:, t0:t0 + TW], we, pv))
        p.fence()
    for b in range(2):
        p.ve(lambda e, b=b: e.tensor_tensor_scan(
            Sseq[b], Wb, scanbuf[b], 0.0, ALU.mult, ALU.add))
    p.fence()
    i = 0
    for b in range(2):
        for k in range(8):
            for tc in range(4):
                t0 = tc * TW
                ps = po[i % 2]
                sb = st_t[i % 2]
                p.mm(ps, ow[k], Sseq[b][:, t0:t0 + TW])
                p.fence()
                p.ve(lambda e, sb=sb, ps=ps: e.tensor_copy(sb, ps))
                p.fence()
                p.dma(s_o[k * 128:(k + 1) * 128, b * S + t0:b * S + t0 + TW], sb)
                i += 1
    p.fence()
    p.finish()
    return nc


def _t(a):
    return np.ascontiguousarray(np.asarray(a, dtype=np.float32))


def run_l1(inp):
    nc = build_l1(_bass())
    x = _t(inp["L0_attn_x"])
    keyT = _t(inp["L0_attn_key_w"].T)
    valT = _t(inp["L0_attn_value_w"].T)
    outT = _t(inp["L0_attn_output_w"].T)
    td = _t(inp["L0_attn_time_decay"]).reshape(64, 1)
    tf = _t(inp["L0_attn_time_first"]).reshape(64, 1)
    tsg = _t(inp["L0_attn_time_shift_gates"]).reshape(D, 1)
    lnw = _t(inp["L0_attn_ln_w"]).reshape(D, 1)
    lnb = _t(inp["L0_attn_ln_b"]).reshape(D, 1)
    in_maps = []
    for c in range(NCORES):
        xT = _t(x[c * BPC:(c + 1) * BPC].reshape(TPC, D).T)
        in_maps.append(dict(xT=xT, keyT=keyT, valT=valT, outT=outT, td=td,
                            tf=tf, tsg=tsg, lnw=lnw, lnb=lnb))
    r = run_bass_kernel_spmd(nc, in_maps, core_ids=list(range(NCORES)))
    return r.results


def build_tx(nc):
    hin = _inp(nc, "hin", (D, TPC))
    WQKV = _inp(nc, "wqkv", (D, 1536))
    QS = _inp(nc, "qsc", (128, 1))
    KS = _inp(nc, "ksc", (128, 1))
    WO = _inp(nc, "wo", (512, D))
    LG = _inp(nc, "lg", (128, 8))
    LB = _inp(nc, "lbc", (128, 8))
    W1P = _inp(nc, "w1p", (22, D, 256))
    B1P = _inp(nc, "b1p", (128, 44))
    W2Q = _inp(nc, "w2q", (8, 2816, 128))
    B2 = _inp(nc, "b2", (128, 8))
    AQ = _inp(nc, "aq", (128, S))
    BQ = _inp(nc, "bq", (128, S))
    AK = _inp(nc, "ak", (128, S))
    BK = _inp(nc, "bk", (128, S))
    P128T = _inp(nc, "p128t", (128, 128))
    IDENT = _inp(nc, "ident", (128, 128))
    BLK2 = _inp(nc, "blk2", (128, 2))
    BLK2T = _inp(nc, "blk2t", (2, 128))
    MT = _inp(nc, "mt", (4, 128, 1024))
    MBT = _inp(nc, "mbt", (8, 128, 512))
    hout = _out(nc, "hout", (D, TPC))
    x1d = _scr(nc, "x1scr", (D, TPC))

    p = P(nc)
    TW = 512
    PS0 = p.psum([128, 2048])
    PS1 = p.psum([128, 2048])
    ones_col = p.sbuf([128, 1]); ones_row = p.sbuf([1, 128])
    ones_col_bf = p.sbuf([128, 1], BF16)
    blk2 = p.sbuf([128, 2]); blk2T = p.sbuf([2, 128])
    ident = p.sbuf([128, 128])
    qs_c = p.sbuf([128, 1]); ks_c = p.sbuf([128, 1]); qs8 = p.sbuf([128, 1])
    b1pt = p.sbuf([128, 44]); b2t = p.sbuf([128, 8])
    lg_t = p.sbuf([128, 8]); lb_t = p.sbuf([128, 8])
    r0 = p.sbuf([1, TW]); r1 = p.sbuf([1, TW]); r2 = p.sbuf([1, TW])
    rn1 = p.sbuf([2, TW]); rn2 = p.sbuf([2, TW])
    Brow = p.sbuf([1, TW]); Bb = p.sbuf([128, TW])
    nb_sb = p.sbuf([128, TW]); rec_sb = p.sbuf([128, TW])
    t0 = p.sbuf([128, TW]); t1 = p.sbuf([128, TW]); t2 = p.sbuf([128, TW])
    t3 = p.sbuf([128, TW]); qsqt = p.sbuf([128, TW])
    e_bf = p.sbuf([128, TW], BF16)
    mcol = p.sbuf([128, 1])
    msim = p.sbuf([128, 1024])
    consts = dict(ones_col=ones_col, ones_row=ones_row, r0=r0, r1=r1, r2=r2)
    p.const_tile(1e-5)
    p.po(lambda e: e.memset(ones_col, 1.0))
    p.po(lambda e: e.memset(ones_row, 1.0))
    p.po(lambda e: e.memset(ones_col_bf, 1.0))
    p.dma(ident, IDENT)
    p.dma(blk2, BLK2); p.dma(blk2T, BLK2T)
    p.dma(qs_c, QS); p.dma(ks_c, KS)
    p.dma(b1pt, B1P); p.dma(b2t, B2); p.dma(lg_t, LG); p.dma(lb_t, LB)
    p.fence()
    p.ve(lambda e: e.tensor_scalar(qs8, qs_c, 8.0, None, ALU.mult))
    p.fence()
    wq_bf = [p.sbuf_scoped([128, 1536]) for _ in range(8)]
    wo_bf = [p.sbuf_scoped([128, 1024]) for _ in range(4)]
    aq_t = p.sbuf_scoped([128, TW]); bq_t = p.sbuf_scoped([128, TW])
    ak_t = p.sbuf_scoped([128, TW]); bk_t = p.sbuf_scoped([128, TW])
    p128_t = p.sbuf_scoped([128, 128])
    mt_t = [p.sbuf_scoped([128, 1024], BF16) for _ in range(4)]
    mbt_t = [p.sbuf_scoped([128, TW], BF16) for _ in range(8)]
    hs_bf = [p.sbuf_scoped([128, TW]) for _ in range(8)]
    qp = [p.sbuf_scoped([128, 1024]) for _ in range(4)]
    kp = [p.sbuf_scoped([128, 1024]) for _ in range(4)]
    vT = [p.sbuf_scoped([128, TW]) for _ in range(8)]
    obuf = [p.sbuf_scoped([128, TW]) for _ in range(4)]
    for k in range(8):
        p.dma(wq_bf[k], WQKV[k * 128:(k + 1) * 128, :])
    for k in range(4):
        p.dma(wo_bf[k], WO[k * 128:(k + 1) * 128, :])
    p.dma(p128_t, P128T)
    for i in range(4):
        p.dma(mt_t[i], MT[i], eng="gpsimd")
    for i in range(8):
        p.dma(mbt_t[i], MBT[i], eng="gpsimd")
    p.fence()

    psA = PS0[:, 0:512]; psB = PS0[:, 512:1024]
    pn = PS1[0:2, 0:512]; pbn = PS1[:, 512:1024]; prot = PS1[:, 1024:1536]
    psim = PS0[:, 0:1024]; pT = PS0[:, 1024:1536]; poh = PS0[0:64, 1536:2048]
    ptr = PS1[0:1, 1536:1664]; pbB = PS1[:, 512:1024]
    pden = PS1[0:1, 1024:1536]; prec = PS1[:, 0:512]
    for b in range(2):
        cb = b * S
        for w in range(4):
            sl = (w % 2) * TW
            slp = ((w - 1) % 2) * TW
            tl = w * TW
            cols = slice(cb + tl, cb + tl + TW)
            p.dma(aq_t, AQ[:, tl:tl + TW]); p.dma(bq_t, BQ[:, tl:tl + TW])
            p.dma(ak_t, AK[:, tl:tl + TW]); p.dma(bk_t, BK[:, tl:tl + TW])
            for k in range(4):
                p.dma(hs_bf[k], hin[k * 128:(k + 1) * 128, cols])
            for k in range(4, 8):
                if w == 0:
                    p.dma(hs_bf[k][:, 1:TW],
                          hin[k * 128:(k + 1) * 128, cb + tl:cb + tl + TW - 1])
                    p.po(lambda e, k=k: e.memset(hs_bf[k][:, 0:1], 0.0))
                else:
                    p.dma(hs_bf[k], hin[k * 128:(k + 1) * 128, cb + tl - 1:cb + tl + TW - 1])
            p.fence()
            for fc in range(8):
                ps = psA if fc % 2 == 0 else psB
                for k in range(8):
                    p.mm(ps, wq_bf[k][:, fc * 128:(fc + 1) * 128], hs_bf[k],
                         start=(k == 0), stop=(k == 7))
                p.fence()
                p.act(qsqt, ps, AF.Square)
                p.fence()
                p.mm(pn, blk2, qsqt)
                p.fence()
                p.act(rn1, pn, AF.Sqrt)
                p.fence()
                p.ve(lambda e: e.tensor_scalar_max(rn1, rn1, 1e-12))
                p.fence()
                p.ve(lambda e: e.reciprocal(rn2, rn1))
                p.fence()
                p.mm(pbn, blk2T, rn2)
                p.fence()
                p.ve(lambda e: e.tensor_copy(nb_sb, pbn))
                p.fence()
                p.ve(lambda e, ps=ps: e.tensor_mul(t0, ps, nb_sb))
                p.fence()
                sc = qs8 if fc < 4 else ks_c
                p.ve(lambda e, sc=sc: e.tensor_scalar(t1, t0, sc, None, ALU.mult))
                p.fence()
                p.mm(prot, p128_t, t1)
                p.fence()
                At = (aq_t if fc < 4 else ak_t)
                Bt = (bq_t if fc < 4 else bk_t)
                p.ve(lambda e, At=At: e.tensor_mul(t2, t1, At))
                p.fence()
                p.ve(lambda e, Bt=Bt: e.tensor_mul(t1, prot, Bt))
                p.fence()
                dst = qp[fc] if fc < 4 else kp[fc - 4]
                p.ve(lambda e, dst=dst, sl=sl: e.tensor_add(dst[:, sl:sl + TW], t2, t1))
                p.fence()
            for ts_ in range(4):
                for k in range(8):
                    p.mm(psA, hs_bf[k][:, ts_ * 128:(ts_ + 1) * 128],
                         wq_bf[k][:, 1024:1536], start=(k == 0), stop=(k == 7))
                p.fence()
                p.ve(lambda e, i=(w % 2) * 4 + ts_: e.tensor_copy(vT[i], psA))
                p.fence()
            jn = 4 if w == 0 else 8
            for h in range(8):
                qt = qp[h // 2][64 * (h % 2):64 * (h % 2) + 64, :]
                kt = kp[h // 2][64 * (h % 2):64 * (h % 2) + 64, :]
                for ic in range(4):
                    for jh in range(jn // 4):
                        ksl = slp if (jn == 8 and jh == 0) else sl
                        p.mm(psim[:, jh * 512:(jh + 1) * 512],
                             qt[:, sl + ic * 128:sl + (ic + 1) * 128],
                             kt[:, ksl:ksl + 512])
                    p.fence()
                    moff = 1024 - jn * 128
                    p.ve(lambda e, ic=ic, jn=jn, moff=moff: e.tensor_add(
                        msim[:, 0:jn * 128], psim[:, 0:jn * 128], mt_t[ic][:, moff:1024]))
                    p.fence()
                    p.ve(lambda e, jn=jn: e.reduce_max(mcol, msim[:, 0:jn * 128], axis=AX.X))
                    p.fence()
                    p.transpose(ptr, mcol, ident)
                    p.fence()
                    p.ve(lambda e, ic=ic: e.tensor_copy(Brow[0:1, ic * 128:(ic + 1) * 128],
                                                        ptr[0:1, 0:128]))
                    p.fence()
                p.mm(pbB, ones_row, Brow)
                p.fence()
                p.ve(lambda e: e.tensor_copy(Bb, pbB))
                p.fence()
                for jc in range(jn):
                    if jn == 8 and jc < 4:
                        kcol = slp + jc * 128
                        vti = jc + ((w - 1) % 2) * 4
                    else:
                        jl = jc - (4 if jn == 8 else 0)
                        kcol = sl + jl * 128
                        vti = jl + (w % 2) * 4
                    mbi = jc + (4 if w == 0 else 0)
                    p.mm(pT, kt[:, kcol:kcol + 128], qt[:, sl:sl + TW])
                    p.fence()
                    p.ve(lambda e: e.tensor_sub(t0, pT, Bb))
                    p.fence()
                    p.ve(lambda e, mbi=mbi: e.tensor_add(t1, t0, mbt_t[mbi]))
                    p.fence()
                    p.act(t3, t1, AF.Exp)
                    p.fence()
                    p.mm(pden, ones_col, t3, start=(jc == 0), stop=(jc == jn - 1))
                    p.mm(poh, vT[vti][:, h * 64:(h + 1) * 64], t3,
                         start=(jc == 0), stop=(jc == jn - 1))
                    p.fence()
                p.ve(lambda e: e.tensor_copy(r0, pden))
                p.fence()
                p.ve(lambda e: e.reciprocal(r1, r0))
                p.fence()
                p.mm(prec, ones_row, r1)
                p.fence()
                p.ve(lambda e: e.tensor_copy(rec_sb, prec))
                p.fence()
                p.ve(lambda e, h=h: e.tensor_mul(
                    obuf[h // 2][64 * (h % 2):64 * (h % 2) + 64, :],
                    poh, rec_sb[0:64, :]))
                p.fence()
            for dm in range(8):
                ps = psA if dm % 2 == 0 else psB
                for ik in range(4):
                    p.mm(ps, wo_bf[ik][:, dm * 128:(dm + 1) * 128],
                         obuf[ik], start=(ik == 0), stop=(ik == 3))
                p.dma(t2 if dm % 2 == 0 else t3, hin[dm * 128:(dm + 1) * 128, cols])
                p.fence()
                p.ve(lambda e, ps=ps, dm=dm: e.tensor_add(
                    t0 if dm % 2 == 0 else t1, ps, t2 if dm % 2 == 0 else t3))
                p.fence()
                p.dma(x1d[dm * 128:(dm + 1) * 128, cols], t0 if dm % 2 == 0 else t1)
            p.fence()
    p.release(*wq_bf, *wo_bf, aq_t, bq_t, ak_t, bk_t, p128_t, *mt_t, *mbt_t,
              *hs_bf, *qp, *kp, *vT, *obuf)
    xnb = [p.sbuf_scoped([128, S], BF16) for _ in range(8)]
    hh = [p.sbuf_scoped([128, S], BF16) for _ in range(22)]
    w1t = p.sbuf_scoped([128, 2048], BF16)
    w2t = p.sbuf_scoped([128, 2816], BF16)
    gel = p.sbuf_scoped([128, 1024])
    hsf = [p.sbuf_scoped([128, TW]) for _ in range(8)]
    tmpf = [p.sbuf_scoped([128, TW]) for _ in range(8)]
    x1t = msim[:, 0:1024]
    ob = gel
    pb0 = PS1[:, 0:512]; pb1 = PS1[:, 512:1024]
    pr0 = PS1[0:1, 1024:1536]; pr1 = PS1[0:1, 1536:2048]
    lgc = [lg_t[:, k:k + 1] for k in range(8)]
    lbc = [lb_t[:, k:k + 1] for k in range(8)]
    for b in range(2):
        cb = b * S
        for tc in range(4):
            tl = tc * TW
            cols = slice(cb + tl, cb + tl + TW)
            for k in range(4):
                p.dma(hsf[k], x1d[k * 128:(k + 1) * 128, cols])
            for k in range(4, 8):
                if tc == 0:
                    p.dma(hsf[k][:, 1:TW],
                          x1d[k * 128:(k + 1) * 128, cb:cb + TW - 1])
                    p.po(lambda e, k=k: e.memset(hsf[k][:, 0:1], 0.0))
                else:
                    p.dma(hsf[k], x1d[k * 128:(k + 1) * 128, cb + tl - 1:cb + tl + TW - 1])
            p.fence()
            build_ln(p, hsf, [xnb[k][:, tl:tl + TW] for k in range(8)], consts,
                     tmpf, pb0, pb1, pr0, pr1, 8, TW, gamma=lgc, beta=lbc)
        for j in range(22):
            p.dma(w1t.rearrange("p (c m) -> p c m", c=8),
                  W1P[j].rearrange("(c p) m -> p c m", p=128), eng="gpsimd")
            p.fence()
            for q4 in range(4):
                for c in range(8):
                    p.mm(PS0[:, q4 * 512:(q4 + 1) * 512],
                         w1t[:, c * 256:c * 256 + 128],
                         xnb[c][:, q4 * 512:(q4 + 1) * 512],
                         start=(c == 0), stop=(c == 7))
            for q4 in range(4):
                for c in range(8):
                    p.mm(PS1[:, q4 * 512:(q4 + 1) * 512],
                         w1t[:, c * 256 + 128:c * 256 + 256],
                         xnb[c][:, q4 * 512:(q4 + 1) * 512],
                         start=(c == 0), stop=(c == 7))
            p.fence()
            for half in range(2):
                hsl = slice(half * 1024, (half + 1) * 1024)
                p.act(gel, PS1[:, hsl], AF.Gelu, bias=b1pt[:, 2 * j + 1:2 * j + 2])
                p.fence()
                p.ve(lambda e, j=j, hsl=hsl: e.scalar_tensor_tensor(
                    hh[j][:, hsl], PS0[:, hsl], b1pt[:, 2 * j:2 * j + 1], gel,
                    ALU.add, ALU.mult))
                p.fence()
        for dm in range(8):
            p.dma(w2t.rearrange("p (c n) -> p c n", c=22),
                  W2Q[dm].rearrange("(c p) n -> p c n", p=128), eng="gpsimd")
            p.fence()
            for half in range(2):
                hc0 = slice(half * 1024, (half + 1) * 1024)
                for hc in range(22):
                    for q2 in range(2):
                        p.mm(PS0[:, q2 * 512:(q2 + 1) * 512],
                             w2t[:, hc * 128:(hc + 1) * 128],
                             hh[hc][:, half * 1024 + q2 * 512:half * 1024 + (q2 + 1) * 512],
                             start=(hc == 0), stop=(hc == 21))
                p.dma(x1t, x1d[dm * 128:(dm + 1) * 128, cb + half * 1024:cb + (half + 1) * 1024])
                p.fence()
                p.ve(lambda e, dm=dm: e.scalar_tensor_tensor(
                    ob, PS0[:, 0:1024], b2t[:, dm:dm + 1], x1t, ALU.add, ALU.add))
                p.fence()
                p.dma(hout[dm * 128:(dm + 1) * 128, cb + half * 1024:cb + (half + 1) * 1024], ob)
                p.fence()
    p.fence()
    p.finish()
    return nc


def _tx_host(inp, l):
    DH, WIN = 64, 512
    wqkv = _t(inp["L0_ffn_wqkv"][l])
    qs = _t(inp["L0_ffn_q_scale"][l]).reshape(64, 1)
    ks = _t(inp["L0_ffn_k_scale"][l]).reshape(64, 1)
    wo = _t(inp["L0_ffn_wo"][l])
    lg = _t(inp["L0_ffn_ln_g"][l]).reshape(8, 128).T.copy()
    lb = _t(inp["L0_ffn_ln_b"][l]).reshape(8, 128).T.copy()
    w1 = _t(inp["L0_ffn_w1"][l])
    b1 = _t(inp["L0_ffn_b1"][l])
    w2 = _t(inp["L0_ffn_w2"][l])
    b2 = _t(inp["L0_ffn_b2"][l]).reshape(8, 128).T.copy()
    HID = 2730; HP = 2816
    w1a = np.zeros((D, HP), np.float32); w1a[:, :HID] = w1[:, :HID]
    w1g = np.zeros((D, HP), np.float32); w1g[:, :HID] = w1[:, HID:]
    W1P = np.stack([np.concatenate([w1a[:, j * 128:(j + 1) * 128],
                                    w1g[:, j * 128:(j + 1) * 128]], 1)
                    for j in range(22)]).astype(np.float32)
    b1a = np.zeros((HP,), np.float32); b1a[:HID] = b1[:HID]
    b1g = np.zeros((HP,), np.float32); b1g[:HID] = b1[HID:]
    B1P = np.zeros((128, 44), np.float32)
    for j in range(22):
        B1P[:, 2 * j] = b1a[j * 128:(j + 1) * 128]
        B1P[:, 2 * j + 1] = b1g[j * 128:(j + 1) * 128]
    w2p = np.zeros((HP, D), np.float32); w2p[:HID] = w2
    W2Q = np.stack([np.ascontiguousarray(w2p[:, d * 128:(d + 1) * 128])
                    for d in range(8)]).astype(np.float32)
    half = np.arange(0, DH, 2, dtype=np.float32)
    inv = (1.0 / 10000.0 ** (half / DH)).astype(np.float32)
    t = np.arange(S, dtype=np.float32)
    fr = t[:, None] * inv[None, :]
    fr = np.concatenate([fr, fr], 1)
    base = ((half + 0.4 * DH) / (1.4 * DH)).astype(np.float32)
    power = ((t - S // 2) / float(WIN // 2)).astype(np.float32)
    scale = (base[None, :] ** power[:, None]).astype(np.float32)
    scale = np.concatenate([scale, scale], 1)
    cos, sin = np.cos(fr).astype(np.float32), np.sin(fr).astype(np.float32)
    AQ = (cos * scale).T; BQ = (sin * scale).T
    AK = (cos / scale).T; BK = (sin / scale).T
    dup = lambda a: np.ascontiguousarray(np.concatenate([a, a], 0), dtype=np.float32)
    P64 = np.zeros((DH, DH), np.float32)
    for d_ in range(32):
        P64[d_, d_ + 32] = -1.0
        P64[d_ + 32, d_] = 1.0
    P128 = np.zeros((128, 128), np.float32)
    P128[:64, :64] = P64; P128[64:, 64:] = P64
    il = np.arange(512)
    MT = np.zeros((4, 128, 1024), np.float32)
    for ic in range(4):
        jg = np.arange(1024)
        ii = (ic * 128 + np.arange(128))[:, None]
        prev_ok = ii <= jg[None, :]
        cur_ok = ii >= (jg[None, :] - 512)
        ok = np.where(jg[None, :] < 512, prev_ok, cur_ok)
        MT[ic] = np.where(ok, 0.0, -1e30)
    MBT = np.zeros((8, 128, 512), np.float32)
    for jc in range(8):
        jj = (jc * 128 + np.arange(128)) % 512
        is_prev = jc < 4
        if is_prev:
            ok = il[None, :] <= jj[:, None]
        else:
            ok = il[None, :] >= jj[:, None]
        MBT[jc] = np.where(ok, 0.0, -1e30)
    return dict(wqkv=wqkv, qsc=dup(qs.reshape(64, 1)), ksc=dup(ks.reshape(64, 1)),
                wo=wo, lg=_t(lg), lbc=_t(lb), w1p=W1P, b1p=B1P, w2q=W2Q, b2=b2,
                aq=dup(AQ), bq=dup(BQ), ak=dup(AK), bk=dup(BK),
                p128t=np.ascontiguousarray(P128.T), ident=np.eye(128, dtype=np.float32),
                blk2=np.kron(np.eye(2, dtype=np.float32), np.ones((64, 1), np.float32)),
                blk2t=np.kron(np.eye(2, dtype=np.float32), np.ones((1, 64), np.float32)),
                mt=MT, mbt=MBT)


def run_tx(inp, l, hin_list):
    nc = build_tx(_bass())
    base = _tx_host(inp, l)
    in_maps = [dict(base, hin=_t(hin_list[c])) for c in range(NCORES)]
    r = run_bass_kernel_spmd(nc, in_maps, core_ids=list(range(NCORES)))
    return [np.asarray(x["hout"]) for x in r.results]


def build_cmb(nc):
    xT = _inp(nc, "xT", (D, TPC))
    sT = _inp(nc, "sT", (D, TPC))
    gT = _inp(nc, "gT", (D, TPC))
    x1_o = _out(nc, "x1", (D, TPC))
    xn1_o = _out(nc, "xn1", (D, TPC))
    p = P(nc)
    TW = 512
    xt = [p.sbuf([128, TW]) for _ in range(8)]
    st = [p.sbuf([128, TW]) for _ in range(8)]
    gt = [p.sbuf([128, TW]) for _ in range(8)]
    tmp = [p.sbuf([128, TW]) for _ in range(8)]
    on = [p.sbuf([128, TW]) for _ in range(8)]
    ones_col = p.sbuf([128, 1]); ones_row = p.sbuf([1, 128])
    r0 = p.sbuf([1, TW]); r1 = p.sbuf([1, TW]); r2 = p.sbuf([1, TW])
    consts = dict(ones_col=ones_col, ones_row=ones_row, r0=r0, r1=r1, r2=r2)
    PS = p.psum([128, 2048])
    pb0 = PS[:, 0:512]; pb1 = PS[:, 512:1024]
    pr0 = PS[0:1, 1024:1536]; pr1 = PS[0:1, 1536:2048]
    p.const_tile(1e-5)
    p.po(lambda e: e.memset(ones_col, 1.0))
    p.po(lambda e: e.memset(ones_row, 1.0))
    p.fence()
    for tc in range(8):
        cols = slice(tc * TW, (tc + 1) * TW)
        for k in range(8):
            p.dma(xt[k], xT[k * 128:(k + 1) * 128, cols])
            p.dma(st[k], sT[k * 128:(k + 1) * 128, cols])
            p.dma(gt[k], gT[k * 128:(k + 1) * 128, cols])
        p.fence()
        for k in range(8):
            p.act(tmp[k], gt[k], AF.Silu)
        p.fence()
        for k in range(8):
            p.ve(lambda e, k=k: e.tensor_mul(gt[k], st[k], tmp[k]))
        p.fence()
        for k in range(8):
            p.ve(lambda e, k=k: e.tensor_add(xt[k], xt[k], gt[k]))
        p.fence()
        for k in range(8):
            p.dma(x1_o[k * 128:(k + 1) * 128, cols], xt[k])
        p.fence()
        build_ln(p, xt, on, consts, tmp, pb0, pb1, pr0, pr1, 8, TW)
        for k in range(8):
            p.dma(xn1_o[k * 128:(k + 1) * 128, cols], on[k])
        p.fence()
    p.finish()
    return nc


def build_moe_rt(nc):
    xn1 = _inp(nc, "xn1", (D, TPC))
    GWT = _inp(nc, "gwt", (D, 64))
    IOTA64 = _inp(nc, "iota64", (128, 64))
    UT = _inp(nc, "ut", (128, 128))
    idx_o = _out(nc, "idx8", (TPC, 8))
    pos_o = _out(nc, "pos8", (TPC, 8))
    w8_o = _out(nc, "w8", (TPC, 8))
    cnt_o = _out(nc, "cnt", (1, 64))
    p = P(nc)
    gw = [p.sbuf([128, 64]) for _ in range(8)]
    io64 = p.sbuf([128, 64]); ut_t = p.sbuf([128, 128])
    ones_col = p.sbuf([128, 1]); ones_row = p.sbuf([1, 128])
    et = p.sbuf([128, 64]); rep = p.sbuf([128, 64]); wsel = p.sbuf([128, 64])
    asg = p.sbuf([128, 64]); Cg = p.sbuf([128, 64])
    m8 = p.sbuf([128, 8]); idx8u = p.sbuf([128, 8], U32)
    idx8 = p.sbuf([128, 8]); w8t = p.sbuf([128, 8])
    o8t = p.sbuf([128, 8]); o8n = p.sbuf([128, 8]); o8f = p.sbuf([128, 8])
    mx = p.sbuf([128, 1]); nmx = p.sbuf([128, 1]); tsum = p.sbuf([128, 1])
    rsum = p.sbuf([128, 1])
    oh = p.sbuf([128, 64]); pd = p.sbuf([128, 64])
    pos8 = p.sbuf([128, 8])
    rb = p.sbuf([128, 64])
    ct_sb = p.sbuf([1, 64])
    PS = p.psum([128, 512])
    psl = PS[:, 0:64]; pc1 = PS[:, 64:128]; pct = PS[0:1, 128:192]
    pbc = PS[:, 192:256]
    p.po(lambda e: e.memset(ones_col, 1.0))
    p.po(lambda e: e.memset(ones_row, 1.0))
    p.po(lambda e: e.memset(rb, 0.0))
    for k in range(8):
        p.dma(gw[k], GWT[k * 128:(k + 1) * 128, :])
    p.dma(io64, IOTA64); p.dma(ut_t, UT)
    xt = [p.sbuf([128, 128]) for _ in range(8)]
    p.fence()
    for tc in range(32):
        rows = slice(tc * 128, (tc + 1) * 128)
        for k in range(8):
            p.dma(xt[k], xn1[k * 128:(k + 1) * 128, rows])
        p.fence()
        for k in range(8):
            p.mm(psl, xt[k], gw[k], start=(k == 0), stop=(k == 7))
        p.fence()
        p.ve(lambda e: e.reduce_max(mx, psl, axis=AX.X))
        p.fence()
        p.ve(lambda e: e.tensor_scalar(nmx, mx, -1.0, None, ALU.mult))
        p.fence()
        p.act(et, psl, AF.Exp, bias=nmx)
        p.fence()
        p.ve(lambda e: e.max(out=m8, in_=et))
        p.fence()
        p.ve(lambda e: e.max_index(idx8u, m8, et))
        p.ve(lambda e: e.match_replace(out=rep, in_to_replace=m8, in_values=et,
                                       imm_value=0.0))
        p.fence()
        p.ve(lambda e: e.tensor_copy(idx8, idx8u))
        p.fence()
        p.ve(lambda e: e.tensor_sub(wsel, et, rep))
        p.ve(lambda e: e.reduce_sum(tsum, m8, axis=AX.X))
        p.fence()
        p.ve(lambda e: e.tensor_scalar(asg, wsel, 0.0, None, ALU.is_gt))
        p.ve(lambda e: e.reciprocal(rsum, tsum))
        p.fence()
        p.ve(lambda e: e.tensor_scalar(w8t, m8, rsum, None, ALU.mult))
        p.mm(pc1, ut_t, asg)
        p.mm(pct, ones_col, asg)
        p.fence()
        p.ve(lambda e: e.tensor_add(Cg, pc1, rb))
        p.fence()
        for r in range(8):
            p.ve(lambda e, r=r: e.tensor_scalar(oh, io64, idx8[:, r:r + 1], None,
                                                ALU.is_equal))
            p.fence()
            p.ve(lambda e: e.tensor_mul(pd, oh, Cg))
            p.fence()
            p.ve(lambda e, r=r: e.reduce_sum(pos8[:, r:r + 1], pd, axis=AX.X))
            p.fence()
        p.dma(idx_o[rows, :], idx8)
        p.dma(pos_o[rows, :], pos8)
        p.dma(w8_o[rows, :], w8t)
        p.ve(lambda e: e.tensor_copy(ct_sb, pct))
        p.fence()
        p.mm(pbc, ones_row, ct_sb)
        p.fence()
        p.ve(lambda e: e.tensor_add(rb, rb, pbc))
        p.fence()
    p.dma(cnt_o, rb[0:1, :])
    p.fence()
    p.finish()
    return nc


def build_moe_sc(nc, NCH):
    NT = NCH * 128
    xtok = _inp(nc, "xtok", (NT, D))
    idx8 = _inp(nc, "idx8", (NT, 8))
    pos8 = _inp(nc, "pos8", (NT, 8))
    CNT = _inp(nc, "cnt_all", (8, 64))
    SELV = _inp(nc, "selv", (8, 1))
    IOTA64 = _inp(nc, "iota64", (128, 64))
    PREV = _inp(nc, "prevP", (64 * 256, D))
    dispP = _out(nc, "dispP", (64 * 256, D))
    o8g_o = _out(nc, "o8g", (NT, 8))
    p = P(nc)
    xrow = p.sbuf([128, D])
    it = p.sbuf([128, 8]); pt = p.sbuf([128, 8]); pg = p.sbuf([128, 8])
    og = p.sbuf([128, 8]); ogv = p.sbuf([128, 8]); on_ = p.sbuf([128, 8])
    oid = p.sbuf([128, 8], U32)
    io64 = p.sbuf([128, 64]); oh = p.sbuf([128, 64]); pd = p.sbuf([128, 64])
    cnt_t = p.sbuf([8, 64]); selv_t = p.sbuf([8, 1])
    baseb = p.sbuf([128, 64])
    badd = p.sbuf([128, 1])
    ones_row = p.sbuf([1, 128])
    PS = p.psum([128, 512])
    pb = PS[0:1, 0:64]; pbb = PS[:, 64:128]
    p.po(lambda e: e.memset(ones_row, 1.0))
    p.dma(cnt_t, CNT); p.dma(selv_t, SELV); p.dma(io64, IOTA64)
    p.fence()
    p.mm(pb, selv_t, cnt_t)
    p.fence()
    p.ve(lambda e: e.tensor_copy(baseb[0:1, :], pb))
    p.fence()
    p.mm(pbb, ones_row, baseb[0:1, :])
    p.fence()
    p.ve(lambda e: e.tensor_copy(baseb, pbb))
    p.fence()
    for i in range(128):
        p.dma(dispP[i * 128:(i + 1) * 128, :], PREV[i * 128:(i + 1) * 128, :])
    p.fence()
    for tc in range(NCH):
        rows = slice(tc * 128, (tc + 1) * 128)
        p.dma(xrow, xtok[rows, :])
        p.dma(it, idx8[rows, :])
        p.dma(pt, pos8[rows, :])
        p.fence()
        for r in range(8):
            p.ve(lambda e, r=r: e.tensor_scalar(oh, io64, it[:, r:r + 1], None,
                                                ALU.is_equal))
            p.fence()
            p.ve(lambda e: e.tensor_mul(pd, oh, baseb))
            p.fence()
            p.ve(lambda e, r=r: e.reduce_sum(badd, pd, axis=AX.X))
            p.fence()
            p.ve(lambda e, r=r: e.tensor_add(pg[:, r:r + 1], pt[:, r:r + 1], badd))
            p.fence()
        p.ve(lambda e: e.scalar_tensor_tensor(og, it, 256.0, pg, ALU.mult, ALU.add))
        p.ve(lambda e: e.tensor_scalar(on_, pg, 256.0, None, ALU.is_ge))
        p.fence()
        p.ve(lambda e: e.scalar_tensor_tensor(ogv, on_, 1e9, og, ALU.mult, ALU.add))
        p.fence()
        p.dma(o8g_o[rows, :], ogv)
        p.ve(lambda e: e.tensor_copy(oid, ogv))
        p.fence()
        for r in range(8):
            p.idma(dispP, IndirectOffsetOnAxis(oid[:, r:r + 1], 0), xrow, None,
                   bounds=64 * 256 - 1)
        p.fence()
    p.finish()
    return nc


def build_moe_ex(nc):
    dparts = _inp(nc, "dparts", (8, 8 * 256, D))
    WG = _inp(nc, "wgt", (8, D, 768))
    WU = _inp(nc, "wut", (8, D, 768))
    WD = _inp(nc, "wdt", (8, 768, D))
    IDENT = _inp(nc, "ident", (128, 128))
    y_o = _out(nc, "y", (8 * 256, D))
    p = P(nc)
    disp = [p.sbuf([128, D]) for _ in range(2)]
    dispT = [p.sbuf([128, 256], BF16) for _ in range(8)]
    wgt = [p.sbuf([128, 768], BF16) for _ in range(8)]
    wut = [p.sbuf([128, 768], BF16) for _ in range(8)]
    wdt = [p.sbuf([128, D], BF16) for _ in range(6)]
    gu = [p.sbuf([128, 256], BF16) for _ in range(6)]
    gs = p.sbuf([128, 256])
    ysb = p.sbuf([128, D])
    ident = p.sbuf([128, 128])
    PS = p.psum([128, 2048])
    ptx = PS[:, 0:128]
    pg = PS[:, 256:512]; pu = PS[:, 512:768]
    py = PS[:, 1024:1536]
    p.dma(ident, IDENT)
    p.fence()
    for e_ in range(8):
        r0 = e_ * 256
        for sc in range(2):
            rows = slice(r0 + sc * 128, r0 + (sc + 1) * 128)
            p.dma(disp[sc], dparts[0, rows.start:rows.stop, :], eng="gpsimd")
            p.fence()
            for j in range(1, 8):
                self_fn = (lambda en, sc=sc, j=j, a=rows.start, b=rows.stop:
                           en.dma_start(out=disp[sc], in_=dparts[j, a:b, :],
                                        accum_op=ALU.add))
                p._emit("gpsimd", self_fn, 16)
                p.fence()
        for sc in range(2):
            for hc in range(8):
                p.transpose(ptx, disp[sc][:, hc * 128:(hc + 1) * 128], ident)
                p.fence()
                p.ve(lambda e, hc=hc, sc=sc: e.tensor_copy(
                    dispT[hc][:, sc * 128:(sc + 1) * 128], ptx))
                p.fence()
        for hc in range(8):
            p.dma(wgt[hc], WG[e_, hc * 128:(hc + 1) * 128, :], eng="gpsimd")
            p.dma(wut[hc], WU[e_, hc * 128:(hc + 1) * 128, :], eng="gpsimd")
        for ic in range(6):
            p.dma(wdt[ic], WD[e_, ic * 128:(ic + 1) * 128, :], eng="gpsimd")
        p.fence()
        for ic in range(6):
            for hc in range(8):
                p.mm(pg, wgt[hc][:, ic * 128:(ic + 1) * 128], dispT[hc],
                     start=(hc == 0), stop=(hc == 7))
            for hc in range(8):
                p.mm(pu, wut[hc][:, ic * 128:(ic + 1) * 128], dispT[hc],
                     start=(hc == 0), stop=(hc == 7))
            p.fence()
            p.act(gs, pg, AF.Silu)
            p.fence()
            p.ve(lambda e, ic=ic: e.tensor_mul(gu[ic], gs, pu))
            p.fence()
        for sc in range(2):
            for nh in range(2):
                for ic in range(6):
                    p.mm(py, gu[ic][:, sc * 128:(sc + 1) * 128],
                         wdt[ic][:, nh * 512:(nh + 1) * 512],
                         start=(ic == 0), stop=(ic == 5))
                p.fence()
                p.ve(lambda e, nh=nh: e.tensor_copy(ysb[:, nh * 512:(nh + 1) * 512], py))
                p.fence()
            p.dma(y_o[r0 + sc * 128:r0 + (sc + 1) * 128, :], ysb)
            p.fence()
    p.finish()
    return nc


def build_moe_fin(nc, NCH):
    NT = NCH * 128
    Y = _inp(nc, "yfull", (64 * 256 + 1, D))
    o8 = _inp(nc, "o8", (NT, 8))
    w8 = _inp(nc, "w8", (NT, 8))
    x1 = _inp(nc, "x1", (D, NT))
    s2 = _inp(nc, "s2", (D, NT))
    IDENT = _inp(nc, "ident", (128, 128))
    xout = _out(nc, "xout", (D, NT))
    p = P(nc)
    o8t = p.sbuf([128, 8]); o8m = p.sbuf([128, 8])
    oid = p.sbuf([128, 8], U32)
    w8t = p.sbuf([128, 8])
    zk = p.sbuf([128, D])
    accA = p.sbuf([128, D]); accB = p.sbuf([128, D])
    g2 = p.sbuf([128, D])
    x1t = p.sbuf([128, 128]); s2t = p.sbuf([128, 128])
    tt = p.sbuf([128, 128]); ot = p.sbuf([128, 128])
    ident = p.sbuf([128, 128])
    PS = p.psum([128, 512])
    ptx = PS[:, 0:128]
    p.dma(ident, IDENT)
    p.fence()
    for tc in range(NCH):
        rows = slice(tc * 128, (tc + 1) * 128)
        p.dma(o8t, o8[rows, :])
        p.dma(w8t, w8[rows, :])
        p.fence()
        p.ve(lambda e: e.tensor_scalar_min(o8m, o8t, float(64 * 256)))
        p.fence()
        p.ve(lambda e: e.tensor_copy(oid, o8m))
        p.fence()
        for r in range(8):
            p.idma(zk, None, Y, IndirectOffsetOnAxis(oid[:, r:r + 1], 0))
            p.fence()
            a, b_ = (accA, accB) if r % 2 == 1 else (accB, accA)
            if r == 0:
                p.ve(lambda e: e.tensor_scalar(accA, zk, w8t[:, 0:1], None, ALU.mult))
            else:
                p.ve(lambda e, r=r, a=a, b_=b_: e.scalar_tensor_tensor(
                    b_, zk, w8t[:, r:r + 1], a, ALU.mult, ALU.add))
            p.fence()
        p.act(g2, accB, AF.Silu)
        p.fence()
        for dmc in range(8):
            p.transpose(ptx, g2[:, dmc * 128:(dmc + 1) * 128], ident)
            p.dma(x1t, x1[dmc * 128:(dmc + 1) * 128, rows])
            p.dma(s2t, s2[dmc * 128:(dmc + 1) * 128, rows])
            p.fence()
            p.ve(lambda e: e.tensor_mul(tt, s2t, ptx))
            p.fence()
            p.ve(lambda e: e.tensor_add(ot, tt, x1t))
            p.fence()
            p.dma(xout[dmc * 128:(dmc + 1) * 128, rows], ot)
            p.fence()
    p.finish()
    return nc




def build_inv(nc):
    VTR = _inp(nc, "vtr", (512, 512))
    VTI = _inp(nc, "vti", (512, 512))
    VR = _inp(nc, "vr", (512, 512))
    VI = _inp(nc, "vi", (512, 512))
    I512 = _inp(nc, "i512", (512, 512))
    IDENT = _inp(nc, "ident", (128, 128))
    JROW = _inp(nc, "jrow", (64, 64, 64))
    EY = _inp(nc, "ey64", (64, 64))
    GR_o = _out(nc, "gr", (512, 512))
    GI_o = _out(nc, "gi", (512, 512))
    p = P(nc)
    MR = [p.sbuf([128, 1024]) for _ in range(4)]
    MI = [p.sbuf([128, 1024]) for _ in range(4)]
    ident = p.sbuf([128, 128])
    jrow = [p.sbuf([64, 64]) for _ in range(64)]
    ey = p.sbuf([64, 64])
    ones_row = p.sbuf([1, 128])
    GR = p.sbuf([64, 128])
    GI = p.sbuf([64, 128])
    SR = p.sbuf([64, 128]); SI = p.sbuf([64, 128])
    dcp = p.sbuf([64, 64]); dcp2 = p.sbuf([64, 64])
    colA = p.sbuf([64, 1]); colB = p.sbuf([64, 1]); colC = p.sbuf([64, 1])
    colD = p.sbuf([64, 1]); colE = p.sbuf([64, 1]); colF = p.sbuf([64, 1])
    colG = p.sbuf([64, 1])
    u1 = p.sbuf([64, 128]); u2 = p.sbuf([64, 128])
    Er = p.sbuf([64, 64]); Ei = p.sbuf([64, 64]); nEi = p.sbuf([64, 64])
    RR = p.sbuf([64, 1024]); RI = p.sbuf([64, 1024])
    FrT = p.sbuf([64, 64]); FiT = p.sbuf([64, 64]); nFiT = p.sbuf([64, 64])
    PS = p.psum([128, 2048])
    prow_r = PS[0:64, 0:128]; prow_i = PS[0:64, 128:256]
    pt64 = PS[0:64, 256:320]
    pdel_r = PS[0:64, 1024:1536]; pdel_i = PS[0:64, 1536:2048]
    p.dma(ident, IDENT)
    p.dma(ey, EY)
    p.po(lambda e: e.memset(ones_row, 1.0))
    for j in range(64):
        p.dma(jrow[j], JROW[j])
    for pc in range(4):
        p.dma(MR[pc][:, 0:512], VTR[pc * 128:(pc + 1) * 128, :])
        p.dma(MI[pc][:, 0:512], VTI[pc * 128:(pc + 1) * 128, :])
        p.dma(MR[pc][:, 512:1024], I512[pc * 128:(pc + 1) * 128, :])
        p.po(lambda e, pc=pc: e.memset(MI[pc][:, 512:1024], 0.0))
    p.fence()
    for k in range(8):
        pc = k // 2
        base = 64 * (k % 2)
        c0 = k * 64
        p.ve(lambda e, pc=pc, base=base, c0=c0: e.tensor_copy(
            dcp, MR[pc][base:base + 64, c0:c0 + 64]))
        p.ve(lambda e, pc=pc, base=base, c0=c0: e.tensor_copy(
            dcp2, MI[pc][base:base + 64, c0:c0 + 64]))
        p.fence()
        p.transpose(pt64, dcp, ident[0:64, 0:64])
        p.fence()
        p.ve(lambda e: e.tensor_copy(GR[:, 0:64], pt64))
        p.fence()
        p.transpose(pt64, dcp2, ident[0:64, 0:64])
        p.fence()
        p.ve(lambda e: e.tensor_copy(GI[:, 0:64], pt64))
        p.ve(lambda e: e.tensor_copy(GR[:, 64:128], ey))
        p.po(lambda e: e.memset(GI[:, 64:128], 0.0))
        p.fence()
        for j in range(64):
            p.mm(prow_r, jrow[j], GR)
            p.mm(prow_i, jrow[j], GI)
            p.fence()
            p.ve(lambda e: e.tensor_copy(SR, prow_r))
            p.ve(lambda e: e.tensor_copy(SI, prow_i))
            p.fence()
            pr = SR[:, j:j + 1]; pi = SI[:, j:j + 1]
            p.ve(lambda e, pr=pr: e.tensor_mul(colA, pr, pr))
            p.ve(lambda e, pi=pi: e.tensor_mul(colB, pi, pi))
            p.fence()
            p.ve(lambda e: e.tensor_add(colA, colA, colB))
            p.fence()
            p.ve(lambda e: e.reciprocal(colB, colA))
            p.fence()
            fr = GR[:, j:j + 1]; fi = GI[:, j:j + 1]
            p.ve(lambda e, fr=fr, j=j: e.tensor_sub(colC, fr, ey[:, j:j + 1]))
            p.fence()
            p.ve(lambda e, pr=pr: e.tensor_mul(colD, colC, pr))
            p.ve(lambda e, pi=pi, fi=fi: e.tensor_mul(colE, fi, pi))
            p.ve(lambda e, pi=pi: e.tensor_mul(colF, colC, pi))
            p.ve(lambda e, pr=pr, fi=fi: e.tensor_mul(colG, fi, pr))
            p.fence()
            p.ve(lambda e: e.tensor_add(colD, colD, colE))
            p.ve(lambda e: e.tensor_sub(colG, colG, colF))
            p.fence()
            p.ve(lambda e: e.tensor_mul(colD, colD, colB))
            p.ve(lambda e: e.tensor_mul(colG, colG, colB))
            p.fence()
            p.ve(lambda e: e.tensor_scalar(u1, SR, colD, None, ALU.mult))
            p.ve(lambda e: e.tensor_scalar(u2, SI, colG, None, ALU.mult))
            p.fence()
            p.ve(lambda e: e.tensor_sub(u1, u1, u2))
            p.fence()
            p.ve(lambda e: e.tensor_sub(GR, GR, u1))
            p.fence()
            p.ve(lambda e: e.tensor_scalar(u1, SI, colD, None, ALU.mult))
            p.ve(lambda e: e.tensor_scalar(u2, SR, colG, None, ALU.mult))
            p.fence()
            p.ve(lambda e: e.tensor_add(u1, u1, u2))
            p.fence()
            p.ve(lambda e: e.tensor_sub(GI, GI, u1))
            p.fence()
        p.ve(lambda e: e.tensor_copy(Er, GR[:, 64:128]))
        p.ve(lambda e: e.tensor_copy(Ei, GI[:, 64:128]))
        p.fence()
        p.ve(lambda e: e.tensor_scalar(nEi, Ei, -1.0, None, ALU.mult))
        p.fence()
        p.ve(lambda e, pc=pc, base=base: e.tensor_copy(RR, MR[pc][base:base + 64, :]))
        p.ve(lambda e, pc=pc, base=base: e.tensor_copy(RI, MI[pc][base:base + 64, :]))
        p.fence()
        for nchunk in range(2):
            cs = slice(nchunk * 512, (nchunk + 1) * 512)
            p.mm(pdel_r, Er, RR[:, cs], start=True, stop=False)
            p.mm(pdel_r, nEi, RI[:, cs], start=False, stop=True)
            p.mm(pdel_i, Er, RI[:, cs], start=True, stop=False)
            p.mm(pdel_i, Ei, RR[:, cs], start=False, stop=True)
            p.fence()
            p.ve(lambda e, cs=cs: e.tensor_copy(RR[:, cs], pdel_r))
            p.ve(lambda e, cs=cs: e.tensor_copy(RI[:, cs], pdel_i))
            p.fence()
        for o in range(8):
            if o == k:
                continue
            opc = o // 2
            ob = 64 * (o % 2)
            p.ve(lambda e, opc=opc, ob=ob, c0=c0: e.tensor_copy(
                dcp, MR[opc][ob:ob + 64, c0:c0 + 64]))
            p.ve(lambda e, opc=opc, ob=ob, c0=c0: e.tensor_copy(
                dcp2, MI[opc][ob:ob + 64, c0:c0 + 64]))
            p.fence()
            p.transpose(pt64, dcp, ident[0:64, 0:64])
            p.fence()
            p.ve(lambda e: e.tensor_copy(FrT, pt64))
            p.fence()
            p.transpose(pt64, dcp2, ident[0:64, 0:64])
            p.fence()
            p.ve(lambda e: e.tensor_copy(FiT, pt64))
            p.fence()
            p.ve(lambda e: e.tensor_scalar(nFiT, FiT, -1.0, None, ALU.mult))
            p.fence()
            for nchunk in range(2):
                cs = slice(nchunk * 512, (nchunk + 1) * 512)
                p.mm(pdel_r, FrT, RR[:, cs], start=True, stop=False)
                p.mm(pdel_r, nFiT, RI[:, cs], start=False, stop=True)
                p.mm(pdel_i, FrT, RI[:, cs], start=True, stop=False)
                p.mm(pdel_i, FiT, RR[:, cs], start=False, stop=True)
                p.fence()
                p.ve(lambda e, opc=opc, ob=ob, cs=cs: e.tensor_sub(
                    MR[opc][ob:ob + 64, cs], MR[opc][ob:ob + 64, cs], pdel_r))
                p.fence()
                p.ve(lambda e, opc=opc, ob=ob, cs=cs: e.tensor_sub(
                    MI[opc][ob:ob + 64, cs], MI[opc][ob:ob + 64, cs], pdel_i))
                p.fence()
        p.ve(lambda e, pc=pc, base=base: e.tensor_copy(MR[pc][base:base + 64, :], RR))
        p.ve(lambda e, pc=pc, base=base: e.tensor_copy(MI[pc][base:base + 64, :], RI))
        p.fence()
    vr_t = [p.sbuf([128, 512]) for _ in range(4)]
    vi_t = [p.sbuf([128, 512]) for _ in range(4)]
    nvi_t = [p.sbuf([128, 512]) for _ in range(4)]
    Zr = [p.sbuf([128, 512]) for _ in range(4)]
    Zi = [p.sbuf([128, 512]) for _ in range(4)]
    Gtr = [p.sbuf([128, 512]) for _ in range(4)]
    Gti = [p.sbuf([128, 512]) for _ in range(4)]
    i512t = p.sbuf([128, 512])
    for pc in range(4):
        p.dma(vr_t[pc], VR[pc * 128:(pc + 1) * 128, :])
        p.dma(vi_t[pc], VI[pc * 128:(pc + 1) * 128, :])
    p.fence()
    for pc in range(4):
        p.ve(lambda e, pc=pc: e.tensor_scalar(nvi_t[pc], vi_t[pc], -1.0, None, ALU.mult))
    p.fence()
    pz_r = PS[0:128, 0:512]; pz_i = PS[0:128, 512:1024]
    for _polish in range(1):
        for mj in range(4):
            p.dma(i512t, I512[mj * 128:(mj + 1) * 128, :])
            for nk in range(4):
                p.mm(pz_r, vr_t[nk][:, mj * 128:(mj + 1) * 128],
                     MR[nk][:, 512:1024], start=(nk == 0), stop=False)
                p.mm(pz_i, vr_t[nk][:, mj * 128:(mj + 1) * 128],
                     MI[nk][:, 512:1024], start=(nk == 0), stop=False)
            for nk in range(4):
                p.mm(pz_r, nvi_t[nk][:, mj * 128:(mj + 1) * 128],
                     MI[nk][:, 512:1024], start=False, stop=(nk == 3))
                p.mm(pz_i, vi_t[nk][:, mj * 128:(mj + 1) * 128],
                     MR[nk][:, 512:1024], start=False, stop=(nk == 3))
            p.fence()
            p.ve(lambda e, mj=mj: e.tensor_sub(Zr[mj], i512t, pz_r))
            p.ve(lambda e, mj=mj: e.tensor_scalar(Zi[mj], pz_i, -1.0, None, ALU.mult))
            p.fence()
        for pc in range(4):
            for mc in range(4):
                p.transpose(PS[0:128, 1024:1152], MR[pc][:, 512 + mc * 128:512 + (mc + 1) * 128], ident)
                p.fence()
                p.ve(lambda e, pc=pc, mc=mc: e.tensor_copy(
                    Gtr[mc][:, pc * 128:(pc + 1) * 128], PS[0:128, 1024:1152]))
                p.fence()
                p.transpose(PS[0:128, 1024:1152], MI[pc][:, 512 + mc * 128:512 + (mc + 1) * 128], ident)
                p.fence()
                p.ve(lambda e, pc=pc, mc=mc: e.tensor_copy(
                    Gti[mc][:, pc * 128:(pc + 1) * 128], PS[0:128, 1024:1152]))
                p.fence()
        nGti = [p.sbuf([128, 512]) for _ in range(4)]
        for jc in range(4):
            p.ve(lambda e, jc=jc: e.tensor_scalar(nGti[jc], Gti[jc], -1.0, None, ALU.mult))
        p.fence()
        for nr in range(4):
            for jc in range(4):
                p.mm(pz_r, Gtr[jc][:, nr * 128:(nr + 1) * 128], Zr[jc],
                     start=(jc == 0), stop=False)
                p.mm(pz_i, Gtr[jc][:, nr * 128:(nr + 1) * 128], Zi[jc],
                     start=(jc == 0), stop=False)
            for jc in range(4):
                p.mm(pz_r, nGti[jc][:, nr * 128:(nr + 1) * 128], Zi[jc],
                     start=False, stop=(jc == 3))
                p.mm(pz_i, Gti[jc][:, nr * 128:(nr + 1) * 128], Zr[jc],
                     start=False, stop=(jc == 3))
            p.fence()
            p.ve(lambda e, nr=nr: e.tensor_add(MR[nr][:, 512:1024], MR[nr][:, 512:1024], pz_r))
            p.fence()
            p.ve(lambda e, nr=nr: e.tensor_add(MI[nr][:, 512:1024], MI[nr][:, 512:1024], pz_i))
            p.fence()
    for pc in range(4):
        p.dma(GR_o[pc * 128:(pc + 1) * 128, :], MR[pc][:, 512:1024])
        p.dma(GI_o[pc * 128:(pc + 1) * 128, :], MI[pc][:, 512:1024])
    p.fence()
    p.finish()
    return nc



def build_l5(nc):
    xn1 = _inp(nc, "xn1", (D, TPC))
    WAR = _inp(nc, "war", (D, 512))
    WAI = _inp(nc, "wai", (D, 512))
    WXR = _inp(nc, "wxr", (D, 512))
    WXI = _inp(nc, "wxi", (D, 512))
    GTR = _inp(nc, "gtr", (512, 512))
    GTI = _inp(nc, "gti", (512, 512))
    VTR = _inp(nc, "vtr", (512, 512))
    VTI = _inp(nc, "vti", (512, 512))
    WOT = _inp(nc, "wot", (512, D))
    H0R = _inp(nc, "h0r", (512, 1))
    H0I = _inp(nc, "h0i", (512, 1))
    s2_o = _out(nc, "s2", (D, TPC))
    ar_d = _scr(nc, "ar_d", (512, TPC))
    ai_d = _scr(nc, "ai_d", (512, TPC))
    xcr_d = _scr(nc, "xcr_d", (512, TPC))
    xci_d = _scr(nc, "xci_d", (512, TPC))
    sr_d = _scr(nc, "sr_d", (512, TPC))
    si_d = _scr(nc, "si_d", (512, TPC))
    p = P(nc)
    TW = 512
    PS0 = p.psum([128, 2048])
    t1 = p.sbuf([128, TW]); t2 = p.sbuf([128, TW]); t3 = p.sbuf([128, TW])
    t4 = p.sbuf([128, TW])
    war = [p.sbuf_scoped([128, TW]) for _ in range(8)]
    wai = [p.sbuf_scoped([128, TW]) for _ in range(8)]
    wxr = [p.sbuf_scoped([128, TW]) for _ in range(8)]
    wxi = [p.sbuf_scoped([128, TW]) for _ in range(8)]
    xt = [p.sbuf_scoped([128, TW]) for _ in range(8)]
    for k in range(8):
        p.dma(war[k], WAR[k * 128:(k + 1) * 128, :])
        p.dma(wai[k], WAI[k * 128:(k + 1) * 128, :])
        p.dma(wxr[k], WXR[k * 128:(k + 1) * 128, :])
        p.dma(wxi[k], WXI[k * 128:(k + 1) * 128, :])
    p.fence()
    pAr = PS0[:, 0:512]; pAi = PS0[:, 512:1024]
    pXr = PS0[:, 1024:1536]; pXi = PS0[:, 1536:2048]
    for tc in range(8):
        cols = slice(tc * TW, (tc + 1) * TW)
        for k in range(8):
            p.dma(xt[k], xn1[k * 128:(k + 1) * 128, cols])
        p.fence()
        for mc in range(4):
            ms = slice(mc * 128, (mc + 1) * 128)
            for k in range(8):
                p.mm(pAr, war[k][:, ms], xt[k], start=(k == 0), stop=(k == 7))
            for k in range(8):
                p.mm(pAi, wai[k][:, ms], xt[k], start=(k == 0), stop=(k == 7))
            for k in range(8):
                p.mm(pXr, wxr[k][:, ms], xt[k], start=(k == 0), stop=(k == 7))
            for k in range(8):
                p.mm(pXi, wxi[k][:, ms], xt[k], start=(k == 0), stop=(k == 7))
            p.fence()
            p.act(t1, pAr, AF.Square)
            p.act(t2, pAi, AF.Square)
            p.fence()
            p.ve(lambda e: e.tensor_add(t1, t1, t2))
            p.fence()
            p.act(t2, t1, AF.Sqrt)
            p.fence()
            p.ve(lambda e: e.tensor_scalar(t3, t1, 1.0, None, ALU.add))
            p.fence()
            p.ve(lambda e: e.reciprocal(t4, t3))
            p.fence()
            p.ve(lambda e: e.tensor_mul(t2, t2, t4))
            p.fence()
            p.ve(lambda e: e.tensor_mul(t3, pAr, t2))
            p.fence()
            p.dma(ar_d[ms, cols], t3)
            p.fence()
            p.ve(lambda e: e.tensor_mul(t3, pAi, t2))
            p.fence()
            p.dma(ai_d[ms, cols], t3)
            p.fence()
            p.act(t1, pXr, AF.Square)
            p.act(t2, pXi, AF.Square)
            p.fence()
            p.ve(lambda e: e.tensor_add(t1, t1, t2))
            p.fence()
            p.act(t2, t1, AF.Sqrt)
            p.fence()
            p.ve(lambda e: e.tensor_mul(t3, pXr, t2))
            p.fence()
            p.dma(xcr_d[ms, cols], t3)
            p.fence()
            p.ve(lambda e: e.tensor_mul(t3, pXi, t2))
            p.fence()
            p.dma(xci_d[ms, cols], t3)
            p.fence()
    p.release(*war, *wai, *wxr, *wxi, *xt)
    gtr = [p.sbuf_scoped([128, TW]) for _ in range(4)]
    gti = [p.sbuf_scoped([128, TW]) for _ in range(4)]
    ngti = [p.sbuf_scoped([128, TW]) for _ in range(4)]
    AR = p.sbuf_scoped([128, S]); AI = p.sbuf_scoped([128, S])
    BR = p.sbuf_scoped([128, S]); BI = p.sbuf_scoped([128, S])
    AR2 = p.sbuf_scoped([128, S]); AI2 = p.sbuf_scoped([128, S])
    BR2 = p.sbuf_scoped([128, S]); BI2 = p.sbuf_scoped([128, S])
    u1v = p.sbuf_scoped([128, S]); u2v = p.sbuf_scoped([128, S])
    xrr = p.sbuf_scoped([128, TW]); xri = p.sbuf_scoped([128, TW])
    for k in range(4):
        p.dma(gtr[k], GTR[k * 128:(k + 1) * 128, :])
        p.dma(gti[k], GTI[k * 128:(k + 1) * 128, :])
    p.fence()
    for k in range(4):
        p.ve(lambda e, k=k: e.tensor_scalar(ngti[k], gti[k], -1.0, None, ALU.mult))
    p.fence()
    pur = PS0[:, 0:512]; pui = PS0[:, 512:1024]
    for b in range(2):
        cb = b * S
        for mc in range(4):
            ms = slice(mc * 128, (mc + 1) * 128)
            for tc in range(4):
                cols = slice(cb + tc * TW, cb + (tc + 1) * TW)
                p.dma(AR[:, tc * TW:(tc + 1) * TW], ar_d[ms, cols])
                p.dma(AI[:, tc * TW:(tc + 1) * TW], ai_d[ms, cols])
                for nk in range(4):
                    ns = slice(nk * 128, (nk + 1) * 128)
                    if tc == 0:
                        p.dma(xrr[:, 1:TW], xcr_d[ns, cb:cb + TW - 1])
                        p.dma(xri[:, 1:TW], xci_d[ns, cb:cb + TW - 1])
                        p.dma(xrr[:, 0:1], H0R[ns, :])
                        p.dma(xri[:, 0:1], H0I[ns, :])
                    else:
                        p.dma(xrr, xcr_d[ns, cb + tc * TW - 1:cb + (tc + 1) * TW - 1])
                        p.dma(xri, xci_d[ns, cb + tc * TW - 1:cb + (tc + 1) * TW - 1])
                    p.fence()
                    p.mm(pur, gtr[nk][:, ms], xrr, start=(nk == 0), stop=False)
                    p.mm(pur, ngti[nk][:, ms], xri, start=False, stop=(nk == 3))
                    p.mm(pui, gti[nk][:, ms], xrr, start=(nk == 0), stop=False)
                    p.mm(pui, gtr[nk][:, ms], xri, start=False, stop=(nk == 3))
                    p.fence()
                cs = slice(tc * TW, (tc + 1) * TW)
                p.ve(lambda e, cs=cs: e.tensor_mul(u1v[:, cs], AR[:, cs], pur))
                p.ve(lambda e, cs=cs: e.tensor_mul(u2v[:, cs], AI[:, cs], pui))
                p.fence()
                p.ve(lambda e, cs=cs: e.tensor_sub(BR[:, cs], u1v[:, cs], u2v[:, cs]))
                p.fence()
                p.ve(lambda e, cs=cs: e.tensor_mul(u1v[:, cs], AR[:, cs], pui))
                p.ve(lambda e, cs=cs: e.tensor_mul(u2v[:, cs], AI[:, cs], pur))
                p.fence()
                p.ve(lambda e, cs=cs: e.tensor_add(BI[:, cs], u1v[:, cs], u2v[:, cs]))
                p.fence()
            cur = (AR, AI, BR, BI)
            nxt = (AR2, AI2, BR2, BI2)
            for lv in range(11):
                d = 1 << lv
                car, cai, cbr, cbi = cur
                nar, nai, nbr, nbi = nxt
                N = S
                p.ve(lambda e, car=car, nar=nar, d=d: e.tensor_copy(nar[:, 0:d], car[:, 0:d]))
                p.ve(lambda e, cai=cai, nai=nai, d=d: e.tensor_copy(nai[:, 0:d], cai[:, 0:d]))
                p.ve(lambda e, cbr=cbr, nbr=nbr, d=d: e.tensor_copy(nbr[:, 0:d], cbr[:, 0:d]))
                p.ve(lambda e, cbi=cbi, nbi=nbi, d=d: e.tensor_copy(nbi[:, 0:d], cbi[:, 0:d]))
                p.ve(lambda e, car=car, d=d: e.tensor_mul(u1v[:, d:N], car[:, d:N], car[:, 0:N - d]))
                p.ve(lambda e, cai=cai, d=d: e.tensor_mul(u2v[:, d:N], cai[:, d:N], cai[:, 0:N - d]))
                p.fence()
                p.ve(lambda e, nar=nar, d=d: e.tensor_sub(nar[:, d:N], u1v[:, d:N], u2v[:, d:N]))
                p.fence()
                p.ve(lambda e, car=car, cai=cai, d=d: e.tensor_mul(u1v[:, d:N], car[:, d:N], cai[:, 0:N - d]))
                p.ve(lambda e, cai=cai, car=car, d=d: e.tensor_mul(u2v[:, d:N], cai[:, d:N], car[:, 0:N - d]))
                p.fence()
                p.ve(lambda e, nai=nai, d=d: e.tensor_add(nai[:, d:N], u1v[:, d:N], u2v[:, d:N]))
                p.fence()
                p.ve(lambda e, car=car, cbr=cbr, d=d: e.tensor_mul(u1v[:, d:N], car[:, d:N], cbr[:, 0:N - d]))
                p.ve(lambda e, cai=cai, cbi=cbi, d=d: e.tensor_mul(u2v[:, d:N], cai[:, d:N], cbi[:, 0:N - d]))
                p.fence()
                p.ve(lambda e, d=d: e.tensor_sub(u1v[:, d:N], u1v[:, d:N], u2v[:, d:N]))
                p.fence()
                p.ve(lambda e, nbr=nbr, cbr=cbr, d=d: e.tensor_add(nbr[:, d:N], u1v[:, d:N], cbr[:, d:N]))
                p.fence()
                p.ve(lambda e, car=car, cbi=cbi, d=d: e.tensor_mul(u1v[:, d:N], car[:, d:N], cbi[:, 0:N - d]))
                p.ve(lambda e, cai=cai, cbr=cbr, d=d: e.tensor_mul(u2v[:, d:N], cai[:, d:N], cbr[:, 0:N - d]))
                p.fence()
                p.ve(lambda e, d=d: e.tensor_add(u1v[:, d:N], u1v[:, d:N], u2v[:, d:N]))
                p.fence()
                p.ve(lambda e, nbi=nbi, cbi=cbi, d=d: e.tensor_add(nbi[:, d:N], u1v[:, d:N], cbi[:, d:N]))
                p.fence()
                cur, nxt = nxt, cur
            sbr, sbi = cur[2], cur[3]
            p.dma(sr_d[ms, cb:cb + S], sbr)
            p.dma(si_d[ms, cb:cb + S], sbi)
            p.fence()
    p.release(*gtr, *gti, *ngti, AR, AI, BR, BI, AR2, AI2, BR2, BI2, u1v, u2v,
              xrr, xri)
    vtr = [p.sbuf_scoped([128, TW]) for _ in range(4)]
    nvti = [p.sbuf_scoped([128, TW]) for _ in range(4)]
    srt = [p.sbuf_scoped([128, TW]) for _ in range(4)]
    sit = [p.sbuf_scoped([128, TW]) for _ in range(4)]
    hrt = [p.sbuf_scoped([128, TW]) for _ in range(4)]
    wot = [p.sbuf_scoped([128, D]) for _ in range(4)]
    osb = p.sbuf_scoped([128, TW])
    for k in range(4):
        p.dma(vtr[k], VTR[k * 128:(k + 1) * 128, :])
        p.dma(nvti[k], VTI[k * 128:(k + 1) * 128, :])
        p.dma(wot[k], WOT[k * 128:(k + 1) * 128, :])
    p.fence()
    for k in range(4):
        p.ve(lambda e, k=k: e.tensor_scalar(nvti[k], nvti[k], -1.0, None, ALU.mult))
    p.fence()
    ph = PS0[:, 0:512]; ps2a = PS0[:, 512:1024]; ps2b = PS0[:, 1024:1536]
    for b in range(2):
        cb = b * S
        for tc in range(4):
            cols = slice(cb + tc * TW, cb + (tc + 1) * TW)
            for nk in range(4):
                ns = slice(nk * 128, (nk + 1) * 128)
                p.dma(srt[nk], sr_d[ns, cols])
                p.dma(sit[nk], si_d[ns, cols])
            p.fence()
            for mc in range(4):
                ms = slice(mc * 128, (mc + 1) * 128)
                for nk in range(4):
                    p.mm(ph, vtr[nk][:, ms], srt[nk], start=(nk == 0), stop=False)
                    p.mm(ph, nvti[nk][:, ms], sit[nk], start=False, stop=(nk == 3))
                p.dma(t1, xcr_d[ms, cols])
                p.fence()
                p.ve(lambda e, mc=mc: e.tensor_add(hrt[mc], ph, t1))
                p.fence()
            for dm in range(8):
                ps = ps2a if dm % 2 == 0 else ps2b
                for mc in range(4):
                    p.mm(ps, wot[mc][:, dm * 128:(dm + 1) * 128], hrt[mc],
                         start=(mc == 0), stop=(mc == 3))
                p.fence()
                p.ve(lambda e, ps=ps: e.tensor_copy(osb, ps))
                p.fence()
                p.dma(s2_o[dm * 128:(dm + 1) * 128, cols], osb)
                p.fence()
    p.finish()
    return nc


_NC_CACHE = {}


def _get_nc(name, builder):
    if name not in _NC_CACHE:
        _NC_CACHE[name] = builder(_bass())
    return _NC_CACHE[name]


def _runmaps(nc, in_maps, outs):
    r = run_bass_kernel_spmd(nc, in_maps, core_ids=list(range(NCORES)))
    return [{k: np.asarray(res[k]) for k in outs} for res in r.results]


def kernel(**inp):
    inp = {k: np.asarray(v) for k, v in inp.items()}
    x = _t(inp["L0_attn_x"])
    xT = [_t(x[c * BPC:(c + 1) * BPC].reshape(TPC, D).T) for c in range(NCORES)]

    nc1 = _get_nc("l1", build_l1)
    base1 = dict(keyT=_t(inp["L0_attn_key_w"].T), valT=_t(inp["L0_attn_value_w"].T),
                 outT=_t(inp["L0_attn_output_w"].T),
                 td=_t(inp["L0_attn_time_decay"]).reshape(64, 1),
                 tf=_t(inp["L0_attn_time_first"]).reshape(64, 1),
                 tsg=_t(inp["L0_attn_time_shift_gates"]).reshape(D, 1),
                 lnw=_t(inp["L0_attn_ln_w"]).reshape(D, 1),
                 lnb=_t(inp["L0_attn_ln_b"]).reshape(D, 1))
    r1 = _runmaps(nc1, [dict(base1, xT=xT[c]) for c in range(NCORES)], ["xn", "s"])
    xn = [r["xn"] for r in r1]
    s_rwkv = [r["s"] for r in r1]

    nctx = _get_nc("tx", build_tx)
    h = xn
    for l in range(2):
        basetx = _tx_host(inp, l)
        rtx = _runmaps(nctx, [dict(basetx, hin=_t(h[c])) for c in range(NCORES)],
                       ["hout"])
        h = [r["hout"] for r in rtx]

    nc4 = _get_nc("cmb", build_cmb)
    r4 = _runmaps(nc4, [dict(xT=xT[c], sT=_t(s_rwkv[c]), gT=_t(h[c]))
                        for c in range(NCORES)], ["x1", "xn1"])
    x1 = [r["x1"] for r in r4]
    xn1 = [r["xn1"] for r in r4]

    nc5 = _get_nc("inv", build_inv)
    vr = _t(inp["L1_attn_mat_v_real"]); vi = _t(inp["L1_attn_mat_v_imag"])
    jrow = np.zeros((64, 64, 64), np.float32)
    for j in range(64):
        jrow[j, j, :] = 1.0
    binv = dict(vtr=_t(vr.T), vti=_t(vi.T), vr=vr, vi=vi,
                i512=np.eye(512, dtype=np.float32),
                ident=np.eye(128, dtype=np.float32), jrow=jrow,
                ey64=np.eye(64, dtype=np.float32))
    r5 = _runmaps(nc5, [binv] * NCORES, ["gr", "gi"])
    gtr, gti = r5[0]["gr"], r5[0]["gi"]

    nc6 = _get_nc("l5", build_l5)
    wa = _t(inp["L1_attn_w_a"]); wx = _t(inp["L1_attn_w_x"])
    base6 = dict(war=_t(wa[0::2].T), wai=_t(wa[1::2].T),
                 wxr=_t(wx[0::2].T), wxi=_t(wx[1::2].T),
                 gtr=_t(gtr), gti=_t(gti), vtr=_t(vr.T), vti=_t(vi.T),
                 wot=_t(inp["L1_attn_w_out"].T),
                 h0r=_t(inp["L1_attn_last_hidden_real"]).reshape(512, 1),
                 h0i=_t(inp["L1_attn_last_hidden_imag"]).reshape(512, 1))
    r6 = _runmaps(nc6, [dict(base6, xn1=_t(xn1[c])) for c in range(NCORES)], ["s2"])
    s2 = [r["s2"] for r in r6]

    nc7 = _get_nc("moert", build_moe_rt)
    iota64 = np.tile(np.arange(64, dtype=np.float32), (128, 1))
    ut = np.triu(np.ones((128, 128), np.float32), 1)
    base7 = dict(gwt=_t(inp["L1_ffn_gate_w"].T), iota64=_t(iota64), ut=_t(ut))
    r7 = _runmaps(nc7, [dict(base7, xn1=_t(xn1[c])) for c in range(NCORES)],
                  ["idx8", "pos8", "w8", "cnt"])
    cnt_all = _t(np.concatenate([r["cnt"] for r in r7], 0))

    zero_disp = np.zeros((64 * 256, D), np.float32)
    prevP = [zero_disp] * NCORES
    o8g = [np.zeros((TPC, 8), np.float32) for _ in range(NCORES)]
    xtok = [_t(xn1[c].T) for c in range(NCORES)]
    selvs = []
    for c in range(NCORES):
        sv = np.zeros((8, 1), np.float32); sv[:c] = 1.0
        selvs.append(sv)
    for c0, nch in ((0, 5), (5, 5), (10, 5), (15, 5), (20, 5), (25, 5), (30, 2)):
        ncx = _get_nc(f"moesc{nch}", lambda nc, n=nch: build_moe_sc(nc, n))
        rowsl = slice(c0 * 128, (c0 + nch) * 128)
        maps8 = [dict(xtok=_t(xtok[c][rowsl]), idx8=_t(r7[c]["idx8"][rowsl]),
                      pos8=_t(r7[c]["pos8"][rowsl]), cnt_all=cnt_all,
                      selv=selvs[c], iota64=_t(iota64), prevP=_t(prevP[c]))
                 for c in range(NCORES)]
        r8 = _runmaps(ncx, maps8, ["dispP", "o8g"])
        prevP = [r["dispP"] for r in r8]
        for c in range(NCORES):
            o8g[c][rowsl] = r8[c]["o8g"]
    dispP_all = prevP
    nc9 = _get_nc("moeex", build_moe_ex)
    wg = _t(inp["L1_ffn_w_gate"]); wu = _t(inp["L1_ffn_w_up"])
    wd = _t(inp["L1_ffn_w_down"])
    maps9 = []
    for c in range(NCORES):
        es = slice(c * 8, (c + 1) * 8)
        dparts = np.stack([dp[c * 2048:(c + 1) * 2048] for dp in dispP_all])
        maps9.append(dict(
            dparts=_t(dparts),
            wgt=_t(np.ascontiguousarray(wg[es].transpose(0, 2, 1))),
            wut=_t(np.ascontiguousarray(wu[es].transpose(0, 2, 1))),
            wdt=_t(np.ascontiguousarray(wd[es].transpose(0, 2, 1))),
            ident=np.eye(128, dtype=np.float32)))
    r9 = _runmaps(nc9, maps9, ["y"])
    yfull = np.concatenate([r["y"] for r in r9] + [np.zeros((1, D), np.float32)], 0)
    yfull = _t(yfull)

    xout = [np.zeros((D, TPC), np.float32) for _ in range(NCORES)]
    for c0, nch in ((0, 5), (5, 5), (10, 5), (15, 5), (20, 5), (25, 5), (30, 2)):
        ncx = _get_nc(f"moefin{nch}", lambda nc, n=nch: build_moe_fin(nc, n))
        rowsl = slice(c0 * 128, (c0 + nch) * 128)
        maps10 = [dict(yfull=yfull, o8=_t(o8g[c][rowsl]), w8=_t(r7[c]["w8"][rowsl]),
                       x1=_t(x1[c][:, rowsl]), s2=_t(s2[c][:, rowsl]),
                       ident=np.eye(128, dtype=np.float32)) for c in range(NCORES)]
        r10 = _runmaps(ncx, maps10, ["xout"])
        for c in range(NCORES):
            xout[c][:, rowsl] = r10[c]["xout"]
    r10 = [dict(xout=xout[c]) for c in range(NCORES)]
    out = np.concatenate(
        [r["xout"].reshape(D, TPC).T.reshape(BPC, S, D) for r in r10], 0)
    return np.ascontiguousarray(out, dtype=np.float32)
